# version 01
# baseline (speedup 1.0000x reference)
import numpy as np
import ml_dtypes
from contextlib import ExitStack

import concourse.bass as bass
import concourse.mybir as mybir
from concourse.bass_utils import run_bass_kernel_spmd

F32 = mybir.dt.float32
BF16 = mybir.dt.bfloat16
I32 = mybir.dt.int32
U32 = mybir.dt.uint32
AF = mybir.ActivationFunctionType
ALU = mybir.AluOpType

B, S, D = 4, 2048, 2048
NCORES = 8
TOK = 1024
RSQRT_HD = float(1.0 / np.sqrt(128.0))


class Ctr:

    def __init__(self, nc, es, name):
        self.h = es.enter_context(nc.semaphore(name))
        self.n = 0

    def bump(self, inst, k=1):
        inst.then_inc(self.h, k)
        self.n += k
        return self.n


def bfcast(x):
    return np.ascontiguousarray(x).astype(ml_dtypes.bfloat16)



def build_l0_attn():
    nc = bass.Bass("TRN2", target_bir_lowering=False, debug=False,
                   detect_race_conditions=False)
    xTb = nc.dram_tensor("xTb", [D, S], F32, kind="ExternalInput")
    wqT = nc.dram_tensor("wqT", [D, 1024], F32, kind="ExternalInput")
    wkT = nc.dram_tensor("wkT", [D, 1024], F32, kind="ExternalInput")
    wvT = nc.dram_tensor("wvT", [D, 1024], F32, kind="ExternalInput")
    triu = nc.dram_tensor("triu", [128, 128], F32, kind="ExternalInput")
    onesc = nc.dram_tensor("onesc", [128, 128], F32, kind="ExternalInput")
    attnT = nc.dram_tensor("attnT", [1024, S], BF16, kind="ExternalOutput")
    dbg_q = nc.dram_tensor("dbg_q", [128, S], BF16, kind="ExternalOutput")
    dbg_k = nc.dram_tensor("dbg_k", [128, S], BF16, kind="ExternalOutput")
    dbg_vn = nc.dram_tensor("dbg_vn", [128, S], BF16, kind="ExternalOutput")
    dbg_rc = nc.dram_tensor("dbg_rc", [128, 1024], F32, kind="ExternalOutput")

    es = ExitStack()
    sb = lambda name, shape, dt: es.enter_context(nc.sbuf_tensor(name, shape, dt))
    ps = lambda name, shape, dt: es.enter_context(nc.psum_tensor(name, shape, dt))

    xT = sb("xT", [128, 16 * S], BF16)
    wq = sb("wq", [128, 16 * 1024], BF16)
    wk = sb("wk", [128, 16 * 1024], BF16)
    wv = sb("wv", [128, 16 * 1024], BF16)
    triu_b = sb("triu_b", [128, 128], BF16)
    ones_b = sb("ones_b", [128, 128], BF16)
    qT = sb("qT", [128, S], BF16)
    kT = sb("kT", [128, S], BF16)
    vT = sb("vT", [128, S], BF16)
    vn = sb("vn", [128, S], BF16)
    exps = [sb(f"exps{i}", [128, 1024], BF16) for i in range(2)]
    recip = sb("recip", [128, 1024], F32)
    att_o = sb("att_o", [128, S], BF16)

    ps_a = ps("ps_a", [128, 1024], F32)
    ps_b = ps("ps_b", [128, 1024], F32)
    ps_den = ps("ps_den", [128, 1024], F32)
    ps_att = ps("ps_att", [128, 1024], F32)

    with nc.Block() as blk:
        sems = ExitStack()
        g_ld = Ctr(nc, sems, "g_ld")
        mm = Ctr(nc, sems, "mm")
        ev = Ctr(nc, sems, "ev")
        ex = Ctr(nc, sems, "ex")
        mk = Ctr(nc, sems, "mk")
        tr = Ctr(nc, sems, "tr")
        st = Ctr(nc, sems, "st")

        @blk.gpsimd
        def _(g):
            for k in range(16):
                i = g.dma_start(out=xT[:, k * S:(k + 1) * S],
                                in_=xTb[k * 128:(k + 1) * 128, :])
                g_ld.bump(i, 16)
            for w_sb, w_dr in ((wq, wqT), (wk, wkT), (wv, wvT)):
                for k in range(16):
                    i = g.dma_start(out=w_sb[:, k * 1024:(k + 1) * 1024],
                                    in_=w_dr[k * 128:(k + 1) * 128, :])
                    g_ld.bump(i, 16)
            g_ld.bump(g.dma_start(out=triu_b[:, :], in_=triu[:, :]), 16)
            g_ld.bump(g.dma_start(out=ones_b[:, :], in_=onesc[:, :]), 16)

        LD_ALL = g_ld.n

        for h in range(8):
            ev_base = ev.n

            @blk.tensor
            def _(t, h=h, ev_base=ev_base):
                if h == 0:
                    t.wait_ge(g_ld.h, LD_ALL)
                for pi, w_sb in enumerate((wq, wk, wv)):
                    if pi > 0:
                        t.wait_ge(ev.h, ev_base + pi)
                    elif h > 0:
                        t.wait_ge(ev.h, ev_base)
                    last = None
                    for k in range(16):
                        lhsT = w_sb[:, k * 1024 + h * 128: k * 1024 + (h + 1) * 128]
                        for n in range(4):
                            dst = (ps_a if n < 2 else ps_b)[:, (n % 2) * 512:(n % 2 + 1) * 512]
                            last = t.matmul(dst, lhsT,
                                            xT[:, k * S + n * 512: k * S + (n + 1) * 512],
                                            start=(k == 0), stop=(k == 15))
                    mm.bump(last)

            mm_pbase = mm.n - 3

            @blk.vector
            def _(v, h=h, mm_pbase=mm_pbase):
                for i, dst in enumerate((qT, kT, vT)):
                    v.wait_ge(mm.h, mm_pbase + i + 1)
                    if dst is vT:
                        v.wait_ge(tr.h, tr.n)
                    if i == 0 and h > 0:
                        v.wait_ge(st.h, st.n)
                    v.tensor_copy(dst[:, 0:1024], ps_a[:, :])
                    ev.bump(v.tensor_copy(dst[:, 1024:2048], ps_b[:, :]))

            @blk.sync
            def _(s):
                s.wait_ge(ev.h, ev.n)
                for kt in range(16):
                    i = s.dma_start_transpose(vn[:, kt * 128:(kt + 1) * 128],
                                              vT[:, kt * 128:(kt + 1) * 128])
                    tr.bump(i, 16)

            for qh in range(2):
                kts = list(range(0, 8 * (qh + 1)))
                K = len(kts)
                q0 = qh * 1024
                mm_base = mm.n
                ex_base = ex.n
                mk_base = mk.n
                ev_evac = ev.n
                diag = {kt: kt * 128 - q0 for kt in kts if q0 <= kt * 128 < q0 + 1024}

                def emit_scores(t, i, kt):
                    pss = (ps_a, ps_b)[i % 2]
                    co = max(0, kt * 128 - q0)
                    last = None
                    for n in range(2):
                        lo, hi = max(n * 512, co), (n + 1) * 512
                        if lo >= hi:
                            continue
                        last = t.matmul(pss[:, lo:hi],
                                        kT[:, kt * 128:(kt + 1) * 128],
                                        qT[:, q0 + lo:q0 + hi],
                                        start=True, stop=True)
                    mm.bump(last)

                @blk.tensor
                def _(t, kts=kts, K=K, q0=q0, mm_base=mm_base, ex_base=ex_base,
                      mk_base=mk_base, ev_evac=ev_evac, diag=diag, qh=qh, h=h):
                    if qh == 0 and h == 0:
                        t.wait_ge(tr.h, tr.n)
                    elif qh == 0:
                        t.wait_ge(tr.h, tr.n)
                    t.wait_ge(ev.h, ev_evac)
                    emit_scores(t, 0, kts[0])
                    if K > 1:
                        emit_scores(t, 1, kts[1])
                    ndiag_seen = 0
                    for i, kt in enumerate(kts):
                        t.wait_ge(ex.h, ex_base + i + 1)
                        if kt in diag:
                            ndiag_seen += 1
                            t.wait_ge(mk.h, mk_base + ndiag_seen)
                        first, last_i = (i == 0), (i == K - 1)
                        co = max(0, kt * 128 - q0)
                        last = None
                        for n in range(2):
                            lo, hi = max(n * 512, co), (n + 1) * 512
                            if lo >= hi:
                                continue
                            t.matmul(ps_den[:, lo:hi], ones_b[:, :],
                                     exps[i % 2][:, lo:hi], start=first, stop=last_i)
                            last = t.matmul(ps_att[:, lo:hi],
                                            vn[:, kt * 128:(kt + 1) * 128],
                                            exps[i % 2][:, lo:hi], start=first, stop=last_i)
                        mm.bump(last)
                        if i + 2 < K:
                            emit_scores(t, i + 2, kts[i + 2])

                @blk.scalar
                def _(sc, kts=kts, K=K, q0=q0, mm_base=mm_base, mk_base=mk_base,
                      diag=diag):
                    ndiag_seen = 0
                    for i, kt in enumerate(kts):
                        sc.wait_ge(mm.h, mm_base + (i + 1 if i < 2 else 2 * i))
                        if i >= 2:
                            sc.wait_ge(mm.h, mm_base + 3 + 2 * (i - 2))
                            if kts[i - 2] in diag:
                                ndiag_seen += 1
                                sc.wait_ge(mk.h, mk_base + ndiag_seen)
                        co = max(0, kt * 128 - q0)
                        pss = (ps_a, ps_b)[i % 2]
                        a = sc.activation(exps[i % 2][:, co:1024], pss[:, co:1024],
                                          AF.Exp, scale=RSQRT_HD)
                        ex.bump(a)

                @blk.vector
                def _(v, kts=kts, q0=q0, ex_base=ex_base, diag=diag):
                    for kt, co in diag.items():
                        i = kts.index(kt)
                        v.wait_ge(ex.h, ex_base + i + 1)
                        m = v.tensor_tensor(exps[i % 2][:, co:co + 128],
                                            exps[i % 2][:, co:co + 128],
                                            triu_b[:, :], ALU.mult)
                        mk.bump(m)

                @blk.vector
                def _(v, q0=q0):
                    v.wait_ge(mm.h, mm.n)
                    v.reciprocal(recip[:, :], ps_den[:, :])
                    e = v.tensor_tensor(att_o[:, q0:q0 + 1024], ps_att[:, :],
                                        recip[:, :], ALU.mult)
                    ev.bump(e)

            @blk.sync
            def _(s, h=h):
                s.wait_ge(ev.h, ev.n)
                i = s.dma_start(out=attnT[h * 128:(h + 1) * 128, :],
                                in_=att_o[:, :])
                st.bump(i, 16)
                if h == 0:
                    st.bump(s.dma_start(out=dbg_q[:, :], in_=qT[:, :]), 16)
                    st.bump(s.dma_start(out=dbg_k[:, :], in_=kT[:, :]), 16)
                    st.bump(s.dma_start(out=dbg_vn[:, :], in_=vn[:, :]), 16)
                    st.bump(s.dma_start(out=dbg_rc[:, :], in_=recip[:, :]), 16)

        @blk.sync
        def _(s):
            s.wait_ge(st.h, st.n)

        sems.close()
    es.close()
    return nc


def host_prep_l0_attn(inputs):
    x = inputs["L0_attn_hidden_states"]
    w_qkvT = np.ascontiguousarray(inputs["L0_attn_w_qkv"].T)
    triu = np.ascontiguousarray(np.triu(np.ones((128, 128), np.float32)))
    ones = np.ones((128, 128), np.float32)
    maps = []
    for c in range(NCORES):
        b, half = c // 2, c % 2
        lo = 8 * half * 128
        maps.append({
            "xTb": np.ascontiguousarray(x[b].T),
            "wqT": np.ascontiguousarray(w_qkvT[:, lo:lo + 1024]),
            "wkT": np.ascontiguousarray(w_qkvT[:, 2048 + lo:2048 + lo + 1024]),
            "wvT": np.ascontiguousarray(w_qkvT[:, 4096 + lo:4096 + lo + 1024]),
            "triu": triu,
            "onesc": ones,
        })
    return maps





NIO = 43


def build_l0_ffn(upto="ALL", race=False):
    nc = bass.Bass("TRN2", target_bir_lowering=False, debug=False,
                   detect_race_conditions=race)
    attnTb = nc.dram_tensor("attnTb", [D, TOK], BF16, kind="ExternalInput")
    xresT = nc.dram_tensor("xresT", [D, TOK], F32, kind="ExternalInput")
    woP = nc.dram_tensor("woP", [16, 16, 128, 128], F32, kind="ExternalInput")
    wgP = nc.dram_tensor("wgP", [NIO, 16, 128, 128], BF16, kind="ExternalInput")
    wuP = nc.dram_tensor("wuP", [NIO, 16, 128, 128], BF16, kind="ExternalInput")
    wdP = nc.dram_tensor("wdP", [16, NIO, 128, 128], BF16, kind="ExternalInput")
    invn = nc.dram_tensor("invn", [128, 1], F32, kind="ExternalInput")
    x2T = nc.dram_tensor("x2T", [D, TOK], F32, kind="ExternalOutput")
    x1T = nc.dram_tensor("x1T", [D, TOK], F32)
    rowsD = nc.dram_tensor("rowsD", [2, 512], F32)

    es = ExitStack()
    sb = lambda name, shape, dt: es.enter_context(nc.sbuf_tensor(name, shape, dt))
    ps = lambda name, shape, dt: es.enter_context(nc.psum_tensor(name, shape, dt))

    xr_sb = sb("xr_sb", [128, 16 * 512], F32)
    y_sb = sb("y_sb", [128, 16 * 512], F32)
    ybf = [sb(f"ybf{i}", [128, 512], BF16) for i in range(2)]
    x1b = sb("x1b", [128, 16 * 1024], BF16)
    wo_sb = [sb(f"wo_sb{i}", [128, 16 * 128], BF16) for i in range(2)]
    wg_sb = [sb(f"wg_sb{i}", [128, 16 * 128], BF16) for i in range(2)]
    wu_sb = [sb(f"wu_sb{i}", [128, 16 * 128], BF16) for i in range(2)]
    wd_sb = [sb(f"wd_sb{i}", [128, NIO * 128], BF16) for i in range(2)]
    h_sb = sb("h_sb", [128, NIO * 512], BF16)
    at_sb = h_sb
    sg_sb = sb("sg_sb", [128, 512], BF16)
    sq_sb = sb("sq_sb", [128, 512], BF16)
    inv_sb = sb("inv_sb", [128, 1], BF16)
    r1 = sb("r1", [1, 512], F32)
    r2 = sb("r2", [1, 512], F32)
    rtmp = sb("rtmp", [1, 512], F32)
    mb = sb("mb", [128, 512], F32)
    rbb = sb("rbb", [128, 512], F32)
    x1f = [sb(f"x1f{i}", [128, 512], F32) for i in range(2)]

    ps_y = [ps(f"ps_y{i}", [128, 512], F32) for i in range(2)]
    ps_u = [ps(f"ps_u{i}", [128, 512], F32) for i in range(2)]
    ps_m = ps("ps_m", [1, 512], F32)
    ps_v = ps("ps_v", [1, 512], F32)

    with nc.Block() as blk:
        sems = ExitStack()
        gld = Ctr(nc, sems, "gld")
        sld = Ctr(nc, sems, "sld")
        mm = Ctr(nc, sems, "mm")
        ev = Ctr(nc, sems, "ev")
        ac = Ctr(nc, sems, "ac")
        st = Ctr(nc, sems, "st")
        bc = Ctr(nc, sems, "bc")

        @blk.gpsimd
        def _(g):
            gld.bump(g.dma_start(out=inv_sb[:, :], in_=invn[:, :]), 16)

        def emit_stats(oc, src_bf):
            @blk.scalar
            def _(sc, oc=oc, src_bf=src_bf):
                sc.wait_ge(ev.h, ev.n)
                sc.wait_ge(mm.h, mm.n)
                ac.bump(sc.activation(sq_sb[:, :], src_bf[:, :], AF.Square))

            @blk.tensor
            def _(t, oc=oc, src_bf=src_bf):
                t.wait_ge(ev.h, ev.n)
                t.matmul(ps_m[:, :], inv_sb[:, :], src_bf[:, :],
                         start=(oc == 0), stop=(oc == 15))
                t.wait_ge(ac.h, ac.n)
                mm.bump(t.matmul(ps_v[:, :], inv_sb[:, :], sq_sb[:, :],
                                 start=(oc == 0), stop=(oc == 15)))

        def emit_ln(th, out_bf_full, out_dram):
            @blk.vector
            def _(v):
                v.wait_ge(mm.h, mm.n)
                v.tensor_copy(r1[:, :], ps_m[:, :])
                v.tensor_copy(r2[:, :], ps_v[:, :])
                v.tensor_tensor(rtmp[0:1, :], r1[0:1, :], r1[0:1, :], ALU.mult)
                v.tensor_tensor(r2[0:1, :], r2[0:1, :], rtmp[0:1, :],
                                ALU.subtract)
                ev.bump(v.tensor_scalar(r2[0:1, :], r2[0:1, :], 1e-5, None,
                                        ALU.add))

            @blk.scalar
            def _(sc):
                sc.wait_ge(ev.h, ev.n)
                ac.bump(sc.activation(r2[0:1, :], r2[0:1, :], AF.Sqrt))

            @blk.vector
            def _(v):
                v.wait_ge(ac.h, ac.n)
                ev.bump(v.reciprocal(r2[0:1, :], r2[0:1, :]))

            @blk.gpsimd
            def _(g):
                g.wait_ge(ev.h, ev.n)
                g.dma_start(out=rowsD[0:1, :], in_=r1[0:1, :]).then_inc(bc.h, 16)
                g.dma_start(out=rowsD[1:2, :], in_=r2[0:1, :]).then_inc(bc.h, 16)
                g.wait_ge(bc.h, bc.n + 32)
                bm = bass.AP(tensor=rowsD, offset=0, ap=[[0, 128], [1, 512]])
                br = bass.AP(tensor=rowsD, offset=512, ap=[[0, 128], [1, 512]])
                g.dma_start(out=mb[:, :], in_=bm).then_inc(bc.h, 16)
                g.dma_start(out=rbb[:, :], in_=br).then_inc(bc.h, 16)
            bc.n += 64

            for oc in range(16):
                @blk.vector
                def _(v, oc=oc, th=th):
                    if oc == 0:
                        v.wait_ge(bc.h, bc.n)
                    if oc >= 2:
                        v.wait_ge(st.h, st.n - 16)
                    xf = x1f[oc % 2]
                    v.tensor_tensor(xf[:, :], y_sb[:, oc * 512:(oc + 1) * 512],
                                    mb[:, :], ALU.subtract)
                    v.tensor_tensor(xf[:, :], xf[:, :], rbb[:, :], ALU.mult)
                    ev.bump(v.tensor_copy(
                        out_bf_full[:, oc * 1024 + th * 512: oc * 1024 + th * 512 + 512],
                        xf[:, :]))

                @blk.scalar
                def _(sc, oc=oc, th=th):
                    sc.wait_ge(ev.h, ev.n)
                    st.bump(sc.dma_start(
                        out=out_dram[oc * 128:(oc + 1) * 128, th * 512:th * 512 + 512],
                        in_=x1f[oc % 2][:, :]), 16)

        for th in range(2):
            t0 = th * 512

            @blk.sync
            def _(s, th=th, t0=t0):
                if th == 1:
                    s.wait_ge(mm.h, mm.n)
                for k in range(16):
                    sld.bump(s.dma_start(
                        out=at_sb[:, k * 512:(k + 1) * 512],
                        in_=attnTb[k * 128:(k + 1) * 128, t0:t0 + 512]), 16)
                for k in range(16):
                    sld.bump(s.dma_start(
                        out=xr_sb[:, k * 512:(k + 1) * 512],
                        in_=xresT[k * 128:(k + 1) * 128, t0:t0 + 512]), 16)
            LD_A = sld.n

            @blk.gpsimd
            def _(g, th=th):
                if th == 1:
                    g.wait_ge(mm.h, mm.n)
                for oc in range(2):
                    gld.bump(g.dma_start(
                        out=wo_sb[oc][:, :],
                        in_=woP[oc].rearrange("k p c -> p k c")), 16)
            GP_A = gld.n

            for oc in range(16):
                @blk.gpsimd
                def _(g, oc=oc, th=th):
                    if oc + 2 < 16:
                        g.wait_ge(mm.h, mm.n + 1)
                        gld.bump(g.dma_start(
                            out=wo_sb[oc % 2][:, :],
                            in_=woP[oc + 2].rearrange("k p c -> p k c")), 16)

                @blk.tensor
                def _(t, oc=oc, th=th, GP_A=GP_A, LD_A=LD_A):
                    if oc == 0:
                        t.wait_ge(sld.h, LD_A)
                    t.wait_ge(gld.h, GP_A + 16 * max(0, oc - 1))
                    if oc >= 2:
                        t.wait_ge(ev.h, ev.n)
                    last = None
                    for k in range(16):
                        last = t.matmul(ps_y[oc % 2][:, :],
                                        wo_sb[oc % 2][:, k * 128:(k + 1) * 128],
                                        at_sb[:, k * 512:(k + 1) * 512],
                                        start=(k == 0), stop=(k == 15))
                    mm.bump(last)

                @blk.vector
                def _(v, oc=oc):
                    v.wait_ge(mm.h, mm.n)
                    v.tensor_tensor(y_sb[:, oc * 512:(oc + 1) * 512],
                                    ps_y[oc % 2][:, :],
                                    xr_sb[:, oc * 512:(oc + 1) * 512], ALU.add)
                    ev.bump(v.tensor_copy(ybf[oc % 2][:, :],
                                          y_sb[:, oc * 512:(oc + 1) * 512]))

                if upto != "A0":
                    emit_stats(oc, ybf[oc % 2])

            if upto == "A0":
                for oc in range(16):
                    @blk.scalar
                    def _(sc, oc=oc, th=th):
                        sc.wait_ge(ev.h, ev.n)
                        st.bump(sc.dma_start(
                            out=x2T[oc * 128:(oc + 1) * 128, th * 512:th * 512 + 512],
                            in_=y_sb[:, oc * 512:(oc + 1) * 512]), 16)
                continue
            emit_ln(th, x1b, x1T)
            if upto == "A":
                continue

            @blk.sync
            def _(s, th=th):
                s.wait_ge(st.h, st.n)
                for buf in range(2):
                    sld.bump(s.dma_start(
                        out=wg_sb[buf][:, :],
                        in_=wgP[buf].rearrange("k p c -> p k c")), 16)
                    sld.bump(s.dma_start(
                        out=wu_sb[buf][:, :],
                        in_=wuP[buf].rearrange("k p c -> p k c")), 16)
            LD_B = sld.n

            for io in range(NIO):
                @blk.sync
                def _(s, io=io):
                    if io + 2 < NIO:
                        s.wait_ge(mm.h, mm.n + 2)
                        sld.bump(s.dma_start(
                            out=wg_sb[io % 2][:, :],
                            in_=wgP[io + 2].rearrange("k p c -> p k c")), 16)
                        sld.bump(s.dma_start(
                            out=wu_sb[io % 2][:, :],
                            in_=wuP[io + 2].rearrange("k p c -> p k c")), 16)

                @blk.tensor
                def _(t, io=io, th=th, LD_B=LD_B):
                    t.wait_ge(sld.h, LD_B + 32 * max(0, io - 1))
                    if io >= 2:
                        t.wait_ge(ev.h, ev.n)
                    last = None
                    for k in range(16):
                        last = t.matmul(ps_y[io % 2][:, :],
                                        wg_sb[io % 2][:, k * 128:(k + 1) * 128],
                                        x1b[:, k * 1024 + th * 512: k * 1024 + th * 512 + 512],
                                        start=(k == 0), stop=(k == 15))
                    mm.bump(last)
                    last = None
                    for k in range(16):
                        last = t.matmul(ps_u[io % 2][:, :],
                                        wu_sb[io % 2][:, k * 128:(k + 1) * 128],
                                        x1b[:, k * 1024 + th * 512: k * 1024 + th * 512 + 512],
                                        start=(k == 0), stop=(k == 15))
                    mm.bump(last)

                @blk.scalar
                def _(sc, io=io):
                    sc.wait_ge(mm.h, mm.n - 1)
                    sc.wait_ge(ev.h, ev.n)
                    ac.bump(sc.activation(sg_sb[:, :], ps_y[io % 2][:, :], AF.Silu))

                @blk.vector
                def _(v, io=io):
                    v.wait_ge(mm.h, mm.n)
                    v.wait_ge(ac.h, ac.n)
                    ev.bump(v.tensor_tensor(h_sb[:, io * 512:(io + 1) * 512],
                                            sg_sb[:, :], ps_u[io % 2][:, :], ALU.mult))

            if upto == "B":
                continue
            @blk.sync
            def _(s, th=th):
                s.wait_ge(st.h, st.n)
                for buf in range(2):
                    sld.bump(s.dma_start(
                        out=wd_sb[buf][:, :],
                        in_=wdP[buf].rearrange("k p c -> p k c")), 16)
                for k in range(16):
                    sld.bump(s.dma_start(
                        out=xr_sb[:, k * 512:(k + 1) * 512],
                        in_=x1T[k * 128:(k + 1) * 128, th * 512:th * 512 + 512]), 16)
            LD_B2 = sld.n

            for oc in range(16):
                @blk.sync
                def _(s, oc=oc):
                    if oc + 2 < 16:
                        s.wait_ge(mm.h, mm.n + 1)
                        sld.bump(s.dma_start(
                            out=wd_sb[oc % 2][:, :],
                            in_=wdP[oc + 2].rearrange("k p c -> p k c")), 16)

                @blk.tensor
                def _(t, oc=oc, LD_B2=LD_B2):
                    t.wait_ge(sld.h, LD_B2 + 16 * max(0, oc - 1))
                    if oc == 0:
                        t.wait_ge(ev.h, ev.n)
                    if oc >= 2:
                        t.wait_ge(ev.h, ev.n)
                    last = None
                    for io in range(NIO):
                        last = t.matmul(ps_y[oc % 2][:, :],
                                        wd_sb[oc % 2][:, io * 128:(io + 1) * 128],
                                        h_sb[:, io * 512:(io + 1) * 512],
                                        start=(io == 0), stop=(io == NIO - 1))
                    mm.bump(last)

                @blk.vector
                def _(v, oc=oc):
                    v.wait_ge(mm.h, mm.n)
                    v.tensor_tensor(y_sb[:, oc * 512:(oc + 1) * 512],
                                    ps_y[oc % 2][:, :],
                                    xr_sb[:, oc * 512:(oc + 1) * 512], ALU.add)
                    ev.bump(v.tensor_copy(ybf[oc % 2][:, :],
                                          y_sb[:, oc * 512:(oc + 1) * 512]))

                emit_stats(oc, ybf[oc % 2])

            emit_ln(th, x1b, x2T)

        @blk.scalar
        def _(sc):
            sc.wait_ge(st.h, st.n)

        sems.close()
    es.close()
    return nc


def host_prep_l0_ffn(inputs, attnT_full, WgT, WuT, WdT):
    x = inputs["L0_attn_hidden_states"]
    woT = np.ascontiguousarray(inputs["L0_attn_w_o"].T)
    woP = np.ascontiguousarray(
        woT.reshape(16, 128, 16, 128).transpose(2, 0, 1, 3))
    wg = np.ascontiguousarray(
        WgT.reshape(16, 128, NIO, 128).transpose(2, 0, 1, 3))
    wu = np.ascontiguousarray(
        WuT.reshape(16, 128, NIO, 128).transpose(2, 0, 1, 3))
    wd = np.ascontiguousarray(
        WdT.reshape(NIO, 128, 16, 128).transpose(2, 0, 1, 3))
    invn = np.full((128, 1), 1.0 / 2048.0, np.float32)
    maps = []
    for c in range(NCORES):
        b, half = c // 2, c % 2
        tc = slice(half * 1024, half * 1024 + 1024)
        maps.append({
            "attnTb": np.ascontiguousarray(attnT_full[b][:, tc]),
            "xresT": np.ascontiguousarray(x[b].T[:, tc]),
            "woP": woP, "wgP": wg, "wuP": wu, "wdP": wd, "invn": invn,
        })
    return maps



GC = 5504 // 8
DC = 2048 // 8
UPC = 8192 // 8
DNC = 2048 // 8


def build_wprep():
    nc = bass.Bass("TRN2", target_bir_lowering=False, debug=False,
                   detect_race_conditions=False)
    gidx = nc.dram_tensor("gidx", [D, GC], I32, kind="ExternalInput")
    uidx = nc.dram_tensor("uidx", [D, GC], I32, kind="ExternalInput")
    didx = nc.dram_tensor("didx", [5504, DC], I32, kind="ExternalInput")
    lutb = nc.dram_tensor("lutb", [128, 16], F32, kind="ExternalInput")
    h128 = nc.dram_tensor("h128", [128, 128], F32, kind="ExternalInput")
    wupT = nc.dram_tensor("wupT", [D, UPC], F32, kind="ExternalInput")
    wgtT = nc.dram_tensor("wgtT", [D, UPC], F32, kind="ExternalInput")
    wdnT = nc.dram_tensor("wdnT", [8192, DNC], F32, kind="ExternalInput")

    wg_o = nc.dram_tensor("wg_o", [D, GC], BF16, kind="ExternalOutput")
    wu_o = nc.dram_tensor("wu_o", [D, GC], BF16, kind="ExternalOutput")
    wd_o = nc.dram_tensor("wd_o", [5504, DC], BF16, kind="ExternalOutput")
    wupP = nc.dram_tensor("wupP", [D, UPC], BF16, kind="ExternalOutput")
    wgtP = nc.dram_tensor("wgtP", [D, UPC], BF16, kind="ExternalOutput")
    wdnP = nc.dram_tensor("wdnP", [8192, DNC], BF16, kind="ExternalOutput")

    zD = nc.dram_tensor("zD", [8192, UPC], BF16)

    es = ExitStack()
    sb = lambda name, shape, dt: es.enter_context(nc.sbuf_tensor(name, shape, dt))
    ps = lambda name, shape, dt: es.enter_context(nc.psum_tensor(name, shape, dt))

    idxb = [sb(f"idxb{i}", [128, 1024], BF16) for i in range(2)]
    accb = [sb(f"accb{i}", [128, GC], BF16) for i in range(2)]
    maskb = sb("maskb", [128, GC], BF16)
    lut_sb = sb("lut_sb", [128, 16], F32)
    h128b = sb("h128b", [128, 128], BF16)
    z_sb = sb("z_sb", [128, 16 * 1024], BF16)
    z2_sb = sb("z2_sb", [128, 64 * 256], BF16)
    ps_h = [ps(f"ps_h{i}", [128, 512], F32) for i in range(2)]

    with nc.Block() as blk:
        sems = ExitStack()
        gld = Ctr(nc, sems, "gld")
        mm = Ctr(nc, sems, "mm")
        ev = Ctr(nc, sems, "ev")
        st = Ctr(nc, sems, "st")
        bd = Ctr(nc, sems, "bd")

        @blk.gpsimd
        def _(g):
            gld.bump(g.dma_start(out=lut_sb[:, :], in_=lutb[:, :]), 16)
            gld.bump(g.dma_start(out=h128b[:, :], in_=h128[:, :]), 16)

        ghost_jobs = []
        for (idx_dr, out_dr, ntile, cols) in ((gidx, wg_o, 16, GC),
                                              (uidx, wu_o, 16, GC),
                                              (didx, wd_o, 43, DC)):
            for kt in range(ntile):
                ghost_jobs.append((idx_dr, out_dr, kt, cols))

        for ji, (idx_dr, out_dr, kt, cols) in enumerate(ghost_jobs):
            @blk.gpsimd
            def _(g, ji=ji, idx_dr=idx_dr, kt=kt, cols=cols):
                if ji >= 2:
                    g.wait_ge(ev.h, ji - 1)
                gld.bump(g.dma_start(
                    out=idxb[ji % 2][:, 0:cols],
                    in_=idx_dr[kt * 128:(kt + 1) * 128, :]), 16)

            @blk.vector
            def _(v, ji=ji, cols=cols):
                v.wait_ge(gld.h, 32 + 16 * (ji + 1))
                if ji >= 2:
                    v.wait_ge(st.h, 16 * (ji - 1))
                ib = idxb[ji % 2][:, 0:cols]
                ab = accb[ji % 2][:, 0:cols]
                mk = maskb[:, 0:cols]
                v.tensor_scalar(ab, ib, 0.0, lut_sb[:, 0:1], ALU.is_equal, ALU.mult)
                last = None
                for j in range(1, 16):
                    v.tensor_scalar(mk, ib, float(j), None, ALU.is_equal)
                    last = v.scalar_tensor_tensor(ab, mk, lut_sb[:, j:j + 1], ab,
                                                  ALU.mult, ALU.add)
                ev.bump(last)

            @blk.sync
            def _(s, ji=ji, out_dr=out_dr, kt=kt, cols=cols):
                s.wait_ge(ev.h, ji + 1)
                st.bump(s.dma_start(out=out_dr[kt * 128:(kt + 1) * 128, :],
                                    in_=accb[ji % 2][:, 0:cols]), 16)
        NGHOST = len(ghost_jobs)

        def emit_wprime(w_in, w_out, ntile, cols, nbf_stages):
            for kt in range(ntile):
                @blk.gpsimd
                def _(g, kt=kt, cols=cols, w_in=w_in, evn=ev.n):
                    g.wait_ge(ev.h, evn)
                    g.wait_ge(mm.h, mm.n)
                    gld.bump(g.dma_start(
                        out=idxb[kt % 2][:, 0:cols],
                        in_=w_in[kt * 128:(kt + 1) * 128, :]), 16)

                @blk.tensor
                def _(t, kt=kt, cols=cols):
                    t.wait_ge(gld.h, gld.n)
                    t.wait_ge(ev.h, ev.n)
                    nch = (cols + 511) // 512
                    last = None
                    for n in range(nch):
                        lo, hi = n * 512, min((n + 1) * 512, cols)
                        last = t.matmul(ps_h[n % 2][:, 0:hi - lo], h128b[:, :],
                                        idxb[kt % 2][:, lo:hi], start=True, stop=True)
                    mm.bump(last)

                @blk.vector
                def _(v, kt=kt, cols=cols, stn=st.n):
                    if kt == 0:
                        v.wait_ge(st.h, stn)
                    v.wait_ge(mm.h, mm.n)
                    nch = (cols + 511) // 512
                    lastv = None
                    for n in range(nch):
                        lo, hi = n * 512, min((n + 1) * 512, cols)
                        lastv = v.tensor_copy(z_sb[:, lo:hi], ps_h[n % 2][:, 0:hi - lo])
                    ev.bump(lastv)

                @blk.sync
                def _(s, kt=kt, cols=cols, bdn=bd.n):
                    if kt == 0:
                        s.wait_ge(bd.h, bdn)
                    s.wait_ge(ev.h, ev.n)
                    bd.bump(s.dma_start(out=zD[kt * 128:(kt + 1) * 128, 0:cols],
                                        in_=z_sb[:, 0:cols]), 16)

            @blk.sync
            def _(s, ntile=ntile, cols=cols, stn=st.n):
                s.wait_ge(st.h, stn)
                s.wait_ge(bd.h, bd.n)
                i = s.dma_start(
                    out=z2_sb[:, 0:ntile * cols],
                    in_=zD[0:ntile * 128, 0:cols].rearrange("(j p) o -> p j o", p=128))
                bd.bump(i, 16)

            @blk.vector
            def _(v, ntile=ntile, cols=cols, nbf=nbf_stages):
                v.wait_ge(bd.h, bd.n)
                hstep = 1
                for s_i in range(nbf):
                    ngrp = ntile // (2 * hstep)
                    apA = bass.AP(tensor=z2_sb, offset=0,
                                  ap=[[z2_sb.shape[1], 128],
                                      [2 * hstep * cols, ngrp], [1, hstep * cols]])
                    apB = bass.AP(tensor=z2_sb, offset=hstep * cols,
                                  ap=[[z2_sb.shape[1], 128],
                                      [2 * hstep * cols, ngrp], [1, hstep * cols]])
                    apA2 = bass.AP(tensor=z_sb, offset=0,
                                   ap=[[z_sb.shape[1], 128],
                                       [2 * hstep * cols, ngrp], [1, hstep * cols]])
                    apB2 = bass.AP(tensor=z_sb, offset=hstep * cols,
                                   ap=[[z_sb.shape[1], 128],
                                       [2 * hstep * cols, ngrp], [1, hstep * cols]])
                    srcA, srcB = (apA, apB) if s_i % 2 == 0 else (apA2, apB2)
                    dstA, dstB = (apA2, apB2) if s_i % 2 == 0 else (apA, apB)
                    v.tensor_tensor(dstA, srcA, srcB, ALU.add)
                    last = v.tensor_tensor(dstB, srcA, srcB, ALU.subtract)
                    hstep *= 2
                ev.bump(last)

            final_buf = z2_sb if (nbf_stages % 2 == 0) else z_sb

            @blk.sync
            def _(s, ntile=ntile, cols=cols, w_out=w_out, final_buf=final_buf):
                s.wait_ge(ev.h, ev.n)
                i = s.dma_start(
                    out=w_out[0:ntile * 128, 0:cols].rearrange("(j p) o -> p j o", p=128),
                    in_=bass.AP(tensor=final_buf, offset=0,
                                ap=[[final_buf.shape[1], 128], [cols, ntile], [1, cols]]))
                st.bump(i, 16)

        emit_wprime(wupT, wupP, 16, UPC, 4)
        emit_wprime(wgtT, wgtP, 16, UPC, 4)
        emit_wprime(wdnT, wdnP, 64, DNC, 6)

        @blk.sync
        def _(s):
            s.wait_ge(st.h, st.n)

        sems.close()
    es.close()
    return nc


def host_prep_wprep(inputs):
    gT = np.ascontiguousarray(inputs["L0_ffn_gate_idx"].T)
    uT = np.ascontiguousarray(inputs["L0_ffn_up_idx"].T)
    dT = np.ascontiguousarray(inputs["L0_ffn_down_idx"].T)
    upT = np.ascontiguousarray(inputs["L1_ffn_W_up"].T)
    gtT = np.ascontiguousarray(inputs["L1_ffn_W_gate"].T)
    dnT = np.ascontiguousarray(inputs["L1_ffn_W_down"].T)
    lut = np.ascontiguousarray(
        np.broadcast_to(inputs["L0_ffn_lut"][None, :], (128, 16))).astype(np.float32)
    H = np.array([[1.0]], np.float32)
    while H.shape[0] < 128:
        H = np.block([[H, H], [H, -H]])
    maps = []
    for c in range(NCORES):
        maps.append({
            "gidx": np.ascontiguousarray(gT[:, c * GC:(c + 1) * GC]),
            "uidx": np.ascontiguousarray(uT[:, c * GC:(c + 1) * GC]),
            "didx": np.ascontiguousarray(dT[:, c * DC:(c + 1) * DC]),
            "lutb": lut,
            "h128": np.ascontiguousarray(H),
            "wupT": np.ascontiguousarray(upT[:, c * UPC:(c + 1) * UPC]),
            "wgtT": np.ascontiguousarray(gtT[:, c * UPC:(c + 1) * UPC]),
            "wdnT": np.ascontiguousarray(dnT[:, c * DNC:(c + 1) * DNC]),
        })
    return maps



def build_l1_ffn(dbg=False):
    nc = bass.Bass("TRN2", target_bir_lowering=False, debug=False,
                   detect_race_conditions=False)
    ZTb = nc.dram_tensor("ZTb", [D, TOK], BF16, kind="ExternalInput")
    xresT = nc.dram_tensor("xresT", [D, TOK], F32, kind="ExternalInput")
    woP = nc.dram_tensor("woP", [16, 16, 128, 128], F32, kind="ExternalInput")
    wupP = nc.dram_tensor("wupP", [64, 16, 128, 128], BF16, kind="ExternalInput")
    wgtP = nc.dram_tensor("wgtP", [64, 16, 128, 128], BF16, kind="ExternalInput")
    wdnP = nc.dram_tensor("wdnP", [16, 64, 128, 128], BF16, kind="ExternalInput")
    su_col = nc.dram_tensor("su_col", [128, 16], F32, kind="ExternalInput")
    wsu_col = nc.dram_tensor("wsu_col", [128, 64], F32, kind="ExternalInput")
    wsg_col = nc.dram_tensor("wsg_col", [128, 64], F32, kind="ExternalInput")
    sudp_col = nc.dram_tensor("sudp_col", [128, 64], F32, kind="ExternalInput")
    wsd_col = nc.dram_tensor("wsd_col", [128, 16], F32, kind="ExternalInput")
    invn = nc.dram_tensor("invn", [128, 1], F32, kind="ExternalInput")
    x4T = nc.dram_tensor("x4T", [D, TOK], F32, kind="ExternalOutput")
    x3T = nc.dram_tensor("x3T", [D, TOK], F32, kind="ExternalOutput")
    dbg_h = nc.dram_tensor("dbg_h", [128, 32 * 512], BF16,
                           kind="ExternalOutput" if dbg else "Internal")
    dbg_y0 = nc.dram_tensor("dbg_y0", [128, 16 * 512], F32,
                            kind="ExternalOutput" if dbg else "Internal")
    rowsD = nc.dram_tensor("rowsD", [2, 512], F32)

    es = ExitStack()
    sb = lambda name, shape, dt: es.enter_context(nc.sbuf_tensor(name, shape, dt))
    ps = lambda name, shape, dt: es.enter_context(nc.psum_tensor(name, shape, dt))

    xr_sb = sb("xr_sb", [128, 16 * 512], F32)
    y_sb = sb("y_sb", [128, 16 * 512], F32)
    ybf = [sb(f"ybf{i}", [128, 512], BF16) for i in range(2)]
    x3b = sb("x3b", [128, 16 * 1024], BF16)
    h_sb = sb("h_sb", [128, 32 * 512], BF16)
    at_sb = sb("at_sb", [128, 16 * 512], BF16)
    wo_sb = [sb(f"wo_sb{i}", [128, 16 * 128], BF16) for i in range(2)]
    wg_sb = [sb(f"wg_sb{i}", [128, 16 * 128], BF16) for i in range(2)]
    wu_sb = [sb(f"wu_sb{i}", [128, 16 * 128], BF16) for i in range(2)]
    wd_sb = [sb(f"wd_sb{i}", [128, 32 * 128], BF16) for i in range(2)]
    sg_sb = sb("sg_sb", [128, 512], BF16)
    sq_sb = sb("sq_sb", [128, 512], BF16)
    gsc_sb = sb("gsc_sb", [128, 512], BF16)
    inv_sb = sb("inv_sb", [128, 1], BF16)
    su_sb = sb("su_sb", [128, 16], F32)
    wsu_sb = sb("wsu_sb", [128, 64], F32)
    wsg_sb = sb("wsg_sb", [128, 64], F32)
    sudp_sb = sb("sudp_sb", [128, 64], F32)
    wsd_sb = sb("wsd_sb", [128, 16], F32)
    sup_sb = sb("sup_sb", [128, 64], F32)
    r1 = sb("r1", [1, 512], F32)
    r2 = sb("r2", [1, 512], F32)
    rtmp = sb("rtmp", [1, 512], F32)
    mbb = sb("mbb", [128, 512], F32)
    rbb = sb("rbb", [128, 512], F32)
    x1f = [sb(f"x1f{i}", [128, 512], F32) for i in range(2)]

    ps_y = [ps(f"ps_y{i}", [128, 512], F32) for i in range(2)]
    ps_u = [ps(f"ps_u{i}", [128, 512], F32) for i in range(2)]
    ps_m = ps("ps_m", [1, 512], F32)
    ps_v = ps("ps_v", [1, 512], F32)

    RS2048 = float(1.0 / np.sqrt(2048.0))
    RS8192 = float(1.0 / np.sqrt(8192.0))

    with nc.Block() as blk:
        sems = ExitStack()
        gld = Ctr(nc, sems, "gld")
        sld = Ctr(nc, sems, "sld")
        mm = Ctr(nc, sems, "mm")
        ev = Ctr(nc, sems, "ev")
        ac = Ctr(nc, sems, "ac")
        st = Ctr(nc, sems, "st")
        bc = Ctr(nc, sems, "bc")

        @blk.gpsimd
        def _(g):
            gld.bump(g.dma_start(out=inv_sb[:, :], in_=invn[:, :]), 16)
            for dst, srcd in ((su_sb, su_col), (wsu_sb, wsu_col), (wsg_sb, wsg_col),
                              (sudp_sb, sudp_col), (wsd_sb, wsd_col)):
                gld.bump(g.dma_start(out=dst[:, :], in_=srcd[:, :]), 16)

        @blk.vector
        def _(v):
            v.wait_ge(gld.h, gld.n)
            v.tensor_tensor(sup_sb[:, :], wsu_sb[:, :], sudp_sb[:, :], ALU.mult)
            v.tensor_scalar(sup_sb[:, :], sup_sb[:, :], RS2048, None, ALU.mult)
            v.tensor_scalar(wsg_sb[:, :], wsg_sb[:, :], RS2048, None, ALU.mult)
            ev.bump(v.tensor_scalar(wsd_sb[:, :], wsd_sb[:, :], RS8192, None,
                                    ALU.mult))
        SCALES_EV = ev.n

        def emit_stats(oc, src_bf):
            @blk.scalar
            def _(sc, oc=oc, src_bf=src_bf):
                sc.wait_ge(ev.h, ev.n)
                sc.wait_ge(mm.h, mm.n)
                ac.bump(sc.activation(sq_sb[:, :], src_bf[:, :], AF.Square))

            @blk.tensor
            def _(t, oc=oc, src_bf=src_bf):
                t.wait_ge(ev.h, ev.n)
                t.matmul(ps_m[:, :], inv_sb[:, :], src_bf[:, :],
                         start=(oc == 0), stop=(oc == 15))
                t.wait_ge(ac.h, ac.n)
                mm.bump(t.matmul(ps_v[:, :], inv_sb[:, :], sq_sb[:, :],
                                 start=(oc == 0), stop=(oc == 15)))

        def emit_ln(th, out_mode):
            @blk.vector
            def _(v):
                v.wait_ge(mm.h, mm.n)
                v.tensor_copy(r1[:, :], ps_m[:, :])
                v.tensor_copy(r2[:, :], ps_v[:, :])
                v.tensor_tensor(rtmp[0:1, :], r1[0:1, :], r1[0:1, :], ALU.mult)
                v.tensor_tensor(r2[0:1, :], r2[0:1, :], rtmp[0:1, :], ALU.subtract)
                ev.bump(v.tensor_scalar(r2[0:1, :], r2[0:1, :], 1e-5, None, ALU.add))

            @blk.scalar
            def _(sc):
                sc.wait_ge(ev.h, ev.n)
                ac.bump(sc.activation(r2[0:1, :], r2[0:1, :], AF.Sqrt))

            @blk.vector
            def _(v):
                v.wait_ge(ac.h, ac.n)
                ev.bump(v.reciprocal(r2[0:1, :], r2[0:1, :]))

            @blk.gpsimd
            def _(g):
                g.wait_ge(ev.h, ev.n)
                g.dma_start(out=rowsD[0:1, :], in_=r1[0:1, :]).then_inc(bc.h, 16)
                g.dma_start(out=rowsD[1:2, :], in_=r2[0:1, :]).then_inc(bc.h, 16)
                g.wait_ge(bc.h, bc.n + 32)
                bm = bass.AP(tensor=rowsD, offset=0, ap=[[0, 128], [1, 512]])
                br = bass.AP(tensor=rowsD, offset=512, ap=[[0, 128], [1, 512]])
                g.dma_start(out=mbb[:, :], in_=bm).then_inc(bc.h, 16)
                g.dma_start(out=rbb[:, :], in_=br).then_inc(bc.h, 16)
            bc.n += 64

            for oc in range(16):
                @blk.vector
                def _(v, oc=oc, th=th, out_mode=out_mode):
                    if oc == 0:
                        v.wait_ge(bc.h, bc.n)
                    if oc >= 2:
                        v.wait_ge(st.h, st.n - 16)
                    xf = x1f[oc % 2]
                    v.tensor_tensor(xf[:, :], y_sb[:, oc * 512:(oc + 1) * 512],
                                    mbb[:, :], ALU.subtract)
                    last = v.tensor_tensor(xf[:, :], xf[:, :], rbb[:, :], ALU.mult)
                    if out_mode == "x3":
                        last = v.tensor_scalar(
                            x3b[:, oc * 1024 + th * 512: oc * 1024 + th * 512 + 512],
                            xf[:, :], su_sb[:, oc:oc + 1], None, ALU.mult)
                    ev.bump(last)

                @blk.scalar
                def _(sc, oc=oc, th=th, out_mode=out_mode):
                    sc.wait_ge(ev.h, ev.n)
                    dstd = x3T if out_mode == "x3" else x4T
                    st.bump(sc.dma_start(
                        out=dstd[oc * 128:(oc + 1) * 128, th * 512:th * 512 + 512],
                        in_=x1f[oc % 2][:, :]), 16)

        for th in range(2):
            t0 = th * 512

            @blk.sync
            def _(s, th=th, t0=t0):
                if th == 1:
                    s.wait_ge(mm.h, mm.n)
                for k in range(16):
                    sld.bump(s.dma_start(
                        out=at_sb[:, k * 512:(k + 1) * 512],
                        in_=ZTb[k * 128:(k + 1) * 128, t0:t0 + 512]), 16)
                for k in range(16):
                    sld.bump(s.dma_start(
                        out=xr_sb[:, k * 512:(k + 1) * 512],
                        in_=xresT[k * 128:(k + 1) * 128, t0:t0 + 512]), 16)
            LD_A = sld.n

            @blk.gpsimd
            def _(g, th=th):
                if th == 1:
                    g.wait_ge(mm.h, mm.n)
                for oc in range(2):
                    gld.bump(g.dma_start(
                        out=wo_sb[oc][:, :],
                        in_=woP[oc].rearrange("k p c -> p k c")), 16)
            GP_A = gld.n

            for oc in range(16):
                @blk.gpsimd
                def _(g, oc=oc):
                    if oc + 2 < 16:
                        g.wait_ge(mm.h, mm.n + 1)
                        gld.bump(g.dma_start(
                            out=wo_sb[oc % 2][:, :],
                            in_=woP[oc + 2].rearrange("k p c -> p k c")), 16)

                @blk.tensor
                def _(t, oc=oc, GP_A=GP_A, LD_A=LD_A):
                    if oc == 0:
                        t.wait_ge(sld.h, LD_A)
                    t.wait_ge(gld.h, GP_A + 16 * max(0, oc - 1))
                    if oc >= 2:
                        t.wait_ge(ev.h, ev.n)
                    last = None
                    for k in range(16):
                        last = t.matmul(ps_y[oc % 2][:, :],
                                        wo_sb[oc % 2][:, k * 128:(k + 1) * 128],
                                        at_sb[:, k * 512:(k + 1) * 512],
                                        start=(k == 0), stop=(k == 15))
                    mm.bump(last)

                @blk.vector
                def _(v, oc=oc):
                    v.wait_ge(mm.h, mm.n)
                    v.tensor_tensor(y_sb[:, oc * 512:(oc + 1) * 512],
                                    ps_y[oc % 2][:, :],
                                    xr_sb[:, oc * 512:(oc + 1) * 512], ALU.add)
                    ev.bump(v.tensor_copy(ybf[oc % 2][:, :],
                                          y_sb[:, oc * 512:(oc + 1) * 512]))

                emit_stats(oc, ybf[oc % 2])

            emit_ln(th, "x3")

            for ioh in range(2):
                @blk.sync
                def _(s, ioh=ioh, th=th):
                    s.wait_ge(st.h, st.n)
                    for buf in range(2):
                        sld.bump(s.dma_start(
                            out=wg_sb[buf][:, :],
                            in_=wgtP[ioh * 32 + buf].rearrange("k p c -> p k c")), 16)
                        sld.bump(s.dma_start(
                            out=wu_sb[buf][:, :],
                            in_=wupP[ioh * 32 + buf].rearrange("k p c -> p k c")), 16)
                LD_B = sld.n

                for io in range(32):
                    gio = ioh * 32 + io

                    @blk.sync
                    def _(s, io=io, gio=gio):
                        if io + 2 < 32:
                            s.wait_ge(mm.h, mm.n + 2)
                            sld.bump(s.dma_start(
                                out=wg_sb[io % 2][:, :],
                                in_=wgtP[gio + 2].rearrange("k p c -> p k c")), 16)
                            sld.bump(s.dma_start(
                                out=wu_sb[io % 2][:, :],
                                in_=wupP[gio + 2].rearrange("k p c -> p k c")), 16)

                    @blk.tensor
                    def _(t, io=io, th=th, LD_B=LD_B):
                        t.wait_ge(sld.h, LD_B + 32 * max(0, io - 1))
                        t.wait_ge(ev.h, ev.n)
                        last = None
                        for k in range(16):
                            last = t.matmul(
                                ps_y[io % 2][:, :],
                                wg_sb[io % 2][:, k * 128:(k + 1) * 128],
                                x3b[:, k * 1024 + th * 512: k * 1024 + th * 512 + 512],
                                start=(k == 0), stop=(k == 15))
                        mm.bump(last)
                        last = None
                        for k in range(16):
                            last = t.matmul(
                                ps_u[io % 2][:, :],
                                wu_sb[io % 2][:, k * 128:(k + 1) * 128],
                                x3b[:, k * 1024 + th * 512: k * 1024 + th * 512 + 512],
                                start=(k == 0), stop=(k == 15))
                        mm.bump(last)

                    @blk.vector
                    def _(v, io=io, gio=gio):
                        v.wait_ge(mm.h, mm.n - 1)
                        v.wait_ge(ac.h, ac.n)
                        ev.bump(v.tensor_scalar(gsc_sb[:, :], ps_y[io % 2][:, :],
                                                wsg_sb[:, gio:gio + 1], None,
                                                ALU.mult))

                    @blk.scalar
                    def _(sc, io=io):
                        sc.wait_ge(ev.h, ev.n)
                        ac.bump(sc.activation(sg_sb[:, :], gsc_sb[:, :], AF.Silu))

                    @blk.vector
                    def _(v, io=io, gio=gio):
                        v.wait_ge(mm.h, mm.n)
                        v.wait_ge(ac.h, ac.n)
                        ev.bump(v.scalar_tensor_tensor(
                            h_sb[:, io * 512:(io + 1) * 512],
                            ps_u[io % 2][:, :], sup_sb[:, gio:gio + 1], sg_sb[:, :],
                            ALU.mult, ALU.mult))

                @blk.sync
                def _(s, ioh=ioh):
                    for buf in range(2):
                        sld.bump(s.dma_start(
                            out=wd_sb[buf][:, :],
                            in_=wdnP[buf, ioh * 32:(ioh + 1) * 32].rearrange(
                                "k p c -> p k c")), 16)
                LD_D = sld.n

                for oc in range(16):
                    @blk.sync
                    def _(s, oc=oc, ioh=ioh):
                        if oc + 2 < 16:
                            s.wait_ge(mm.h, mm.n + 1)
                            sld.bump(s.dma_start(
                                out=wd_sb[oc % 2][:, :],
                                in_=wdnP[oc + 2, ioh * 32:(ioh + 1) * 32].rearrange(
                                    "k p c -> p k c")), 16)

                    @blk.tensor
                    def _(t, oc=oc, LD_D=LD_D):
                        t.wait_ge(sld.h, LD_D + 16 * max(0, oc - 1))
                        if oc == 0:
                            t.wait_ge(ev.h, ev.n)
                        if oc >= 2:
                            t.wait_ge(ev.h, ev.n)
                        last = None
                        for io in range(32):
                            last = t.matmul(ps_y[oc % 2][:, :],
                                            wd_sb[oc % 2][:, io * 128:(io + 1) * 128],
                                            h_sb[:, io * 512:(io + 1) * 512],
                                            start=(io == 0), stop=(io == 31))
                        mm.bump(last)

                    @blk.vector
                    def _(v, oc=oc, ioh=ioh):
                        v.wait_ge(mm.h, mm.n)
                        if ioh == 0:
                            ev.bump(v.tensor_copy(y_sb[:, oc * 512:(oc + 1) * 512],
                                                  ps_y[oc % 2][:, :]))
                        else:
                            ev.bump(v.tensor_tensor(y_sb[:, oc * 512:(oc + 1) * 512],
                                                    y_sb[:, oc * 512:(oc + 1) * 512],
                                                    ps_y[oc % 2][:, :], ALU.add))

            if dbg and th == 0:
                @blk.sync
                def _(s):
                    s.wait_ge(ev.h, ev.n)
                    st.bump(s.dma_start(out=dbg_h[:, :], in_=h_sb[:, :]), 16)
                    st.bump(s.dma_start(out=dbg_y0[:, :], in_=y_sb[:, :]), 16)

            @blk.sync
            def _(s, th=th):
                s.wait_ge(st.h, st.n)
                for k in range(16):
                    sld.bump(s.dma_start(
                        out=xr_sb[:, k * 512:(k + 1) * 512],
                        in_=x3T[k * 128:(k + 1) * 128, th * 512:th * 512 + 512]), 16)
            LD_R = sld.n

            for oc in range(16):
                @blk.vector
                def _(v, oc=oc, LD_R=LD_R):
                    if oc == 0:
                        v.wait_ge(sld.h, LD_R)
                        v.wait_ge(st.h, st.n)
                    if oc >= 2:
                        v.wait_ge(mm.h, mm.n - 1)
                    v.scalar_tensor_tensor(
                        y_sb[:, oc * 512:(oc + 1) * 512],
                        y_sb[:, oc * 512:(oc + 1) * 512], wsd_sb[:, oc:oc + 1],
                        xr_sb[:, oc * 512:(oc + 1) * 512], ALU.mult, ALU.add)
                    ev.bump(v.tensor_copy(ybf[oc % 2][:, :],
                                          y_sb[:, oc * 512:(oc + 1) * 512]))

                emit_stats(oc, ybf[oc % 2])

            emit_ln(th, "x4")

        @blk.scalar
        def _(sc):
            sc.wait_ge(st.h, st.n)

        sems.close()
    es.close()
    return nc


def host_prep_l1_ffn(inputs, ZT_full, x2_full):
    woT = np.ascontiguousarray(inputs["L1_attn_Wo"].T)
    woP = np.ascontiguousarray(woT.reshape(16, 128, 16, 128).transpose(2, 0, 1, 3))
    invn = np.full((128, 1), 1.0 / 2048.0, np.float32)
    su_col = np.ascontiguousarray(
        inputs["L1_ffn_SU_ug"].reshape(16, 128).T).astype(np.float32)
    wsu_col = np.ascontiguousarray(
        inputs["L1_ffn_Wscale_ug"][:8192].reshape(64, 128).T).astype(np.float32)
    wsg_col = np.ascontiguousarray(
        inputs["L1_ffn_Wscale_ug"][8192:].reshape(64, 128).T).astype(np.float32)
    sudp_col = np.ascontiguousarray(
        inputs["L1_ffn_SU_dp"].reshape(64, 128).T).astype(np.float32)
    wsd_col = np.ascontiguousarray(
        inputs["L1_ffn_Wscale_dp"].reshape(16, 128).T).astype(np.float32)
    maps = []
    for c in range(NCORES):
        b, half = c // 2, c % 2
        tc = slice(half * 1024, half * 1024 + 1024)
        maps.append({
            "ZTb": np.ascontiguousarray(ZT_full[b][:, tc]),
            "xresT": np.ascontiguousarray(x2_full[b][:, tc]),
            "woP": woP, "invn": invn,
            "su_col": su_col, "wsu_col": wsu_col, "wsg_col": wsg_col,
            "sudp_col": sudp_col, "wsd_col": wsd_col,
        })
    return maps


def panelize_l1_w(wupP_full, wgtP_full, wdnP_full):
    up = np.ascontiguousarray(
        wupP_full.reshape(16, 128, 64, 128).transpose(2, 0, 1, 3))
    gt = np.ascontiguousarray(
        wgtP_full.reshape(16, 128, 64, 128).transpose(2, 0, 1, 3))
    dn = np.ascontiguousarray(
        wdnP_full.reshape(64, 128, 16, 128).transpose(2, 0, 1, 3))
    return up, gt, dn



def build_l1_attn(nch=128, SCAN_PARTS=5):
    nc = bass.Bass("TRN2", target_bir_lowering=False, debug=False,
                   detect_race_conditions=False)
    x2Tb = nc.dram_tensor("x2Tb", [D, S], F32, kind="ExternalInput")
    wqT = nc.dram_tensor("wqT", [D, 1024], F32, kind="ExternalInput")
    wkT = nc.dram_tensor("wkT", [D, 1024], F32, kind="ExternalInput")
    wvT = nc.dram_tensor("wvT", [D, 1024], F32, kind="ExternalInput")
    wilrT = nc.dram_tensor("wilrT", [D, 16], F32, kind="ExternalInput")
    bilr = nc.dram_tensor("bilr", [16, 1], F32, kind="ExternalInput")
    w1in = nc.dram_tensor("w1in", [128, 512], F32, kind="ExternalInput")
    pen = nc.dram_tensor("pen", [1, S], F32, kind="ExternalInput")
    ident = nc.dram_tensor("ident", [128, 128], F32, kind="ExternalInput")
    trimask = nc.dram_tensor("trimask", [128, 128], F32, kind="ExternalInput")
    Zraw = nc.dram_tensor("Zraw", [128, 128 * 128], BF16, kind="ExternalOutput")
    ilrD = nc.dram_tensor("ilrD", [16, S], F32)

    es = ExitStack()
    sb = lambda name, shape, dt: es.enter_context(nc.sbuf_tensor(name, shape, dt))
    ps = lambda name, shape, dt: es.enter_context(nc.psum_tensor(name, shape, dt))

    xb_t = sb("xb_t", [128, 8 * S], BF16)
    xc_t = sb("xc_t", [128, 8 * S], BF16)
    xa_t = sb("xa_t", [128, 8 * S], BF16)
    alias = sb("alias", [128, 16 * S], BF16)
    ilr_sb = sb("ilr_sb", [32, S], F32)
    penb = sb("penb", [32, S], F32)
    s_sb = sb("s_sb", [128, S], F32)
    w1i_sb = sb("w1i_sb", [128, 512], F32)
    id_f = sb("id_f", [128, 128], F32)
    id_b = sb("id_b", [128, 128], BF16)
    tri_b = sb("tri_b", [128, 128], BF16)
    bilr_sb = sb("bilr_sb", [16, 1], F32)
    wq_sb = sb("wq_sb", [128, 16 * 128], BF16)
    wk_sb = sb("wk_sb", [128, 16 * 128], BF16)
    wv_sb = sb("wv_sb", [128, 16 * 128], BF16)
    wi_sb = sb("wi_sb", [128, 16 * 16], BF16)
    w1sb = sb("w1sb", [128, 512], BF16)
    z1sb = sb("z1sb", [128, 256], BF16)
    z1n = sb("z1n", [128, 256], BF16)
    xbn = sb("xbn", [128, 256], BF16)
    at_s = sb("at_s", [128, 128], BF16)
    gbuf = sb("gbuf", [128, 256], BF16)
    zbuf = [sb(f"zbuf{i}", [128, 128], BF16) for i in range(2)]

    ps_w1 = ps("ps_w1", [128, 512], F32)
    ps_z1 = ps("ps_z1", [128, 512], F32)
    ps_zb = ps("ps_zb", [128, 512], F32)
    ps_at = ps("ps_at", [128, 512], F32)
    ps_az = ps("ps_az", [128, 512], F32)
    ps_t1 = ps("ps_t1", [128, 512], BF16)
    ps_t2 = ps("ps_t2", [128, 512], BF16)

    XCp = alias
    XBp_OFF = 8 * S

    with nc.Block() as blk:
        sems = ExitStack()
        gld = Ctr(nc, sems, "gld")
        mm = Ctr(nc, sems, "mm")
        ev = Ctr(nc, sems, "ev")
        ac = Ctr(nc, sems, "ac")
        st = Ctr(nc, sems, "st")
        bc = Ctr(nc, sems, "bc")

        @blk.gpsimd
        def _(g):
            for k in range(16):
                gld.bump(g.dma_start(out=alias[:, k * S:(k + 1) * S],
                                     in_=x2Tb[k * 128:(k + 1) * 128, :]), 16)
            gld.bump(g.dma_start(out=w1i_sb[:, :], in_=w1in[:, :]), 16)
            gld.bump(g.dma_start(out=id_f[:, :], in_=ident[:, :]), 16)
            gld.bump(g.dma_start(out=id_b[:, :], in_=ident[:, :]), 16)
            gld.bump(g.dma_start(out=tri_b[:, :], in_=trimask[:, :]), 16)
            gld.bump(g.dma_start(out=bilr_sb[:, :], in_=bilr[:, :]), 16)
            pb = bass.AP(tensor=pen, offset=0, ap=[[0, 32], [1, S]])
            gld.bump(g.dma_start(out=penb[:, :], in_=pb), 16)
            for k in range(16):
                gld.bump(g.dma_start(out=wi_sb[:, k * 16:(k + 1) * 16],
                                     in_=wilrT[k * 128:(k + 1) * 128, :]), 16)
        LD0 = gld.n

        for p in range(8):
            @blk.gpsimd
            def _(g, p=p):
                if p > 0:
                    g.wait_ge(mm.h, mm.n)
                for w_sb, w_dr in ((wq_sb, wqT), (wk_sb, wkT), (wv_sb, wvT)):
                    for k in range(16):
                        gld.bump(g.dma_start(
                            out=w_sb[:, k * 128:(k + 1) * 128],
                            in_=w_dr[k * 128:(k + 1) * 128,
                                     p * 128:(p + 1) * 128]), 16)

            for wi, (w_sb, dst) in enumerate(((wq_sb, xc_t), (wk_sb, xb_t),
                                              (wv_sb, xa_t))):
                for nh2 in range(2):
                    @blk.tensor
                    def _(t, p=p, wi=wi, w_sb=w_sb, nh2=nh2):
                        t.wait_ge(gld.h, gld.n)
                        if not (p == 0 and wi == 0 and nh2 == 0):
                            t.wait_ge(ev.h, ev.n)
                        last = None
                        for k in range(16):
                            for n in range(2):
                                c0 = k * S + nh2 * 1024 + n * 512
                                last = t.matmul((ps_z1 if n == 0 else ps_zb)[:, 0:512],
                                                w_sb[:, k * 128:(k + 1) * 128],
                                                alias[:, c0:c0 + 512],
                                                start=(k == 0), stop=(k == 15))
                        mm.bump(last)

                    @blk.vector
                    def _(v, p=p, dst=dst, nh2=nh2):
                        v.wait_ge(mm.h, mm.n)
                        c0 = p * S + nh2 * 1024
                        v.tensor_copy(dst[:, c0:c0 + 512], ps_z1[:, 0:512])
                        ev.bump(v.tensor_copy(dst[:, c0 + 512:c0 + 1024],
                                              ps_zb[:, 0:512]))

        for ch4 in range(4):
            @blk.tensor
            def _(t, ch4=ch4):
                t.wait_ge(ev.h, ev.n)
                t.wait_ge(ac.h, ac.n)
                last = None
                for k in range(16):
                    c0 = k * S + ch4 * 512
                    last = t.matmul(ps_z1[0:16, 0:512],
                                    wi_sb[:, k * 16:(k + 1) * 16],
                                    alias[:, c0:c0 + 512],
                                    start=(k == 0), stop=(k == 15))
                mm.bump(last)

            @blk.scalar
            def _(sc, ch4=ch4):
                sc.wait_ge(mm.h, mm.n)
                ac.bump(sc.activation(ilr_sb[0:16, ch4 * 512:(ch4 + 1) * 512],
                                      ps_z1[0:16, 0:512], AF.Sigmoid,
                                      bias=bilr_sb[0:16, 0:1]))

        @blk.vector
        def _(v):
            v.wait_ge(ac.h, ac.n)
            ev.bump(v.tensor_tensor(ilr_sb[0:16, :], ilr_sb[0:16, :],
                                    penb[0:16, :], ALU.mult))

        @blk.gpsimd
        def _(g):
            g.wait_ge(ev.h, ev.n)
            bc.bump(g.dma_start(out=ilrD[:, :], in_=ilr_sb[0:16, :]), 16)

        for p in range(8):
            @blk.gpsimd
            def _(g, p=p):
                g.wait_ge(bc.h, 16)
                if p == 0:
                    g.wait_ge(ev.h, ev.n)
                else:
                    g.wait_ge(ev.h, ev.n)
                b0 = bass.AP(tensor=ilrD, offset=(2 * p) * S, ap=[[0, 64], [1, S]])
                b1 = bass.AP(tensor=ilrD, offset=(2 * p + 1) * S, ap=[[0, 64], [1, S]])
                bc.bump(g.dma_start(out=s_sb[0:64, :], in_=b0), 16)
                bc.bump(g.dma_start(out=s_sb[64:128, :], in_=b1), 16)

            @blk.vector
            def _(v, p=p):
                v.wait_ge(bc.h, bc.n)
                v.tensor_tensor(XCp[:, p * S:(p + 1) * S], xc_t[:, p * S:(p + 1) * S],
                                s_sb[:, :], ALU.mult)
                ev.bump(v.scalar_tensor_tensor(
                    alias[:, XBp_OFF + p * S: XBp_OFF + (p + 1) * S],
                    xb_t[:, p * S:(p + 1) * S], -1.0, s_sb[:, :],
                    ALU.mult, ALU.mult))

        @blk.vector
        def _(v):
            v.wait_ge(ev.h, ev.n)
            v.memset(gbuf[:, :], 0.0)
            ev.bump(v.memset(z1sb[:, :], 0.0))

        @blk.tensor
        def _(t):
            t.wait_ge(gld.h, LD0)
            t.wait_ge(ev.h, ev.n)
            mm.bump(t.matmul(ps_w1[:, :], id_f[:, :], w1i_sb[:, :],
                             start=True, stop=False, skip_group_check=True))

        def head_coords(p, s):
            return (p % 4) * 32, (p // 4) * 32 + s * 16

        for ch in range(nch):
            @blk.vector
            def _(v, ch=ch):
                v.wait_ge(mm.h, mm.n)
                v.tensor_copy(w1sb[:, :], ps_w1[:, :])
                last = None
                for g4 in range(2):
                    srcg = bass.AP(tensor=alias, offset=XBp_OFF + (g4 * 4) * S + ch * 16,
                                   ap=[[16 * S, 128], [S, 4], [1, 16]])
                    dstp = bass.AP(tensor=gbuf, offset=g4 * 128,
                                   ap=[[256, 128], [32, 4], [1, 16]])
                    last = v.tensor_copy(dstp, srcg)
                ev.bump(last)
            ev_w1 = ev.n

            @blk.tensor
            def _(t, ch=ch, ev_w1=ev_w1):
                t.wait_ge(ev.h, ev_w1)
                for _ in range(8):
                    t.nop()
                last = None
                for s2 in range(2):
                    for _ in range(8):
                        t.nop()
                    for p in range(8):
                        lhsT = w1sb[s2 * 64:(s2 + 1) * 64, p * 64:(p + 1) * 64]
                        t.matmul(ps_z1[s2 * 64:(s2 + 1) * 64, p * 32:p * 32 + 16],
                                 lhsT,
                                 xb_t[s2 * 64:(s2 + 1) * 64, p * S + ch * 16:
                                      p * S + ch * 16 + 16],
                                 start=True, stop=True, skip_group_check=True,
                                 tile_position=(s2 * 64, s2 * 64))
                        last = t.matmul(
                            ps_zb[s2 * 64:(s2 + 1) * 64, p * 32:p * 32 + 16],
                            lhsT,
                            xc_t[s2 * 64:(s2 + 1) * 64, p * S + ch * 16:
                                 p * S + ch * 16 + 16],
                            start=True, stop=True, skip_group_check=True,
                            tile_position=(s2 * 64, s2 * 64))
                mm.bump(last)
            mm_12 = mm.n

            @blk.vector
            def _(v, ch=ch, mm_12=mm_12):
                v.wait_ge(mm.h, mm_12)
                srcz = bass.AP(tensor=ps_z1, offset=0,
                               ap=[[512, 128], [32, 8], [1, 16]])
                srca = bass.AP(tensor=xa_t, offset=ch * 16,
                               ap=[[8 * S, 128], [S, 8], [1, 16]])
                dstz = bass.AP(tensor=z1sb, offset=0,
                               ap=[[256, 128], [32, 8], [1, 16]])
                ev.bump(v.tensor_tensor(dstz, srcz, srca, ALU.subtract))
            ev_z1 = ev.n

            if SCAN_PARTS >= 3:
                pass
            if SCAN_PARTS >= 3:
                @blk.tensor
                def _(t, ch=ch):
                    last = None
                    for s2 in range(2):
                        for _ in range(8):
                            t.nop()
                        for p in range(8):
                            pb_, cb = head_coords(p, s2)
                            last = t.matmul(
                                ps_at[pb_:pb_ + 16, cb:cb + 16],
                                xb_t[s2 * 64:(s2 + 1) * 64, p * S + ch * 16:
                                     p * S + ch * 16 + 16],
                                XCp[s2 * 64:(s2 + 1) * 64, p * S + ch * 16:
                                    p * S + ch * 16 + 16],
                                start=True, stop=True, skip_group_check=True,
                                tile_position=(s2 * 64, pb_))
                    mm.bump(last)
                mm_3 = mm.n

                if SCAN_PARTS >= 3.5:
                    @blk.vector
                    def _(v, ch=ch, mm_3=mm_3):
                        v.wait_ge(mm.h, mm_3)
                        last = None
                        for pb_ in (0, 32, 64, 96):
                            last = v.tensor_tensor(at_s[pb_:pb_ + 16, 0:64],
                                                   ps_at[pb_:pb_ + 16, 0:64],
                                                   tri_b[pb_:pb_ + 16, 0:64], ALU.mult)
                        ev.bump(last)
                ev_mk = ev.n

            if SCAN_PARTS >= 2:
                @blk.tensor
                def _(t, ch=ch, ev_z1=ev_z1):
                    t.wait_ge(ev.h, ev_z1)
                    for _ in range(8):
                        t.nop()
                    t.transpose(ps_t2[:, 0:128], gbuf[:, 0:128], id_b[:, :])
                    t.transpose(ps_t2[:, 128:256], gbuf[:, 128:256], id_b[:, :])
                    t.transpose(ps_t1[:, 0:128], z1sb[:, 0:128], id_b[:, :])
                    last = t.transpose(ps_t1[:, 128:256], z1sb[:, 128:256], id_b[:, :])
                    mm.bump(last)
                mm_tr = mm.n

                @blk.vector
                def _(v, ch=ch, mm_tr=mm_tr):
                    v.wait_ge(mm.h, mm_tr)
                    v.tensor_copy(xbn[:, :], ps_t2[:, 0:256])
                    ev.bump(v.tensor_copy(z1n[:, :], ps_t1[:, 0:256]))
                ev_tr = ev.n

            if SCAN_PARTS >= 4:
                @blk.tensor
                def _(t, ch=ch, ev_tr=ev_tr, ev_mk=ev_mk):
                    t.wait_ge(ev.h, max(ev_tr, ev_mk))
                    for _ in range(8):
                        t.nop()
                    last = None
                    for po in range(4):
                        for _ in range(8):
                            t.nop()
                        for g4 in range(2):
                            p = g4 * 4 + po
                            for s2 in range(2):
                                pb_, cb = head_coords(p, s2)
                                last = t.matmul(
                                    ps_az[s2 * 64:(s2 + 1) * 64, p * 32:p * 32 + 16],
                                z1n[po * 32:po * 32 + 16,
                                    g4 * 128 + s2 * 64:g4 * 128 + (s2 + 1) * 64],
                                at_s[pb_:pb_ + 16, cb:cb + 16],
                                start=True, stop=True, skip_group_check=True,
                                tile_position=(po * 32, s2 * 64))
                    mm.bump(last)
                mm_4 = mm.n

            if SCAN_PARTS >= 5:
                @blk.tensor
                def _(t, ch=ch):
                    last = None
                    for po in range(4):
                        for _ in range(8):
                            t.nop()
                        for g4 in range(2):
                            p = g4 * 4 + po
                            for s2 in range(2):
                                last = t.matmul(
                                    ps_w1[s2 * 64:(s2 + 1) * 64, p * 64:(p + 1) * 64],
                                xbn[po * 32:po * 32 + 16,
                                    g4 * 128 + s2 * 64:g4 * 128 + (s2 + 1) * 64],
                                z1n[po * 32:po * 32 + 16,
                                    g4 * 128 + s2 * 64:g4 * 128 + (s2 + 1) * 64],
                                start=False, stop=False, skip_group_check=True,
                                tile_position=(po * 32, s2 * 64))
                    mm.bump(last)

            mm_pre_evac = mm.n

            @blk.vector
            def _(v, ch=ch, mm_pre_evac=mm_pre_evac):
                v.wait_ge(mm.h, mm_pre_evac)
                if ch >= 2:
                    v.wait_ge(st.h, 16 * (ch - 1))
                srcz = bass.AP(tensor=ps_zb, offset=0,
                               ap=[[512, 128], [32, 8], [1, 16]])
                srca2 = bass.AP(tensor=ps_az, offset=0,
                                ap=[[512, 128], [32, 8], [1, 16]])
                dstz = bass.AP(tensor=zbuf[ch % 2], offset=0,
                               ap=[[128, 128], [16, 8], [1, 16]])
                v.tensor_copy(dstz, srcz)
                if SCAN_PARTS >= 4:
                    ev.bump(v.tensor_tensor(dstz, dstz, srca2, ALU.add))
                else:
                    ev.bump(v.tensor_copy(dstz, srcz))
            @blk.sync
            def _(s, ch=ch):
                s.wait_ge(ev.h, ev.n)
                st.bump(s.dma_start(out=Zraw[:, ch * 128:(ch + 1) * 128],
                                    in_=zbuf[ch % 2][:, :]), 16)

        @blk.sync
        def _(s):
            s.wait_ge(st.h, st.n)

        sems.close()
    es.close()
    return nc


def host_prep_l1_attn(inputs, x2_full):
    wqT = np.ascontiguousarray(inputs["L1_attn_Wq"].T)
    wkT = np.ascontiguousarray(inputs["L1_attn_Wk"].T)
    wvT = np.ascontiguousarray(inputs["L1_attn_Wv"].T)
    wiT = np.ascontiguousarray(inputs["L1_attn_W_ilr"].T)
    bilr = inputs["L1_attn_b_ilr"].astype(np.float32)
    W1 = inputs["L1_attn_W1"].astype(np.float32)
    pen = (1.0 / (np.arange(S) % 16 + 1.0) / 64.0).astype(np.float32)[None, :]
    ident = np.eye(128, dtype=np.float32)
    tm = np.zeros((128, 128), np.float32)
    for p in range(8):
        for s2 in range(2):
            pb_, cb = (p % 4) * 32, (p // 4) * 32 + s2 * 16
            for d_ in range(16):
                for c_ in range(16):
                    if d_ <= c_:
                        tm[pb_ + d_, cb + c_] = -1.0
    maps = []
    for c in range(NCORES):
        b, half = c // 2, c % 2
        g0 = 16 * half
        cols = slice(g0 * 64, g0 * 64 + 1024)
        w1v = np.zeros((128, 512), np.float32)
        for p in range(8):
            for s2 in range(2):
                w1v[s2 * 64:(s2 + 1) * 64, p * 64:(p + 1) * 64] = W1[g0 + 2 * p + s2]
        maps.append({
            "x2Tb": np.ascontiguousarray(x2_full[b]),
            "wqT": np.ascontiguousarray(wqT[:, cols]),
            "wkT": np.ascontiguousarray(wkT[:, cols]),
            "wvT": np.ascontiguousarray(wvT[:, cols]),
            "wilrT": np.ascontiguousarray(wiT[:, g0:g0 + 16]),
            "bilr": np.ascontiguousarray(bilr[g0:g0 + 16].reshape(16, 1)),
            "w1in": w1v,
            "pen": pen, "ident": ident, "trimask": tm,
        })
    return maps


def zraw_to_zt(zraw):
    z = np.asarray(zraw).reshape(128, 128, 8, 16)
    return np.ascontiguousarray(z.transpose(2, 0, 1, 3).reshape(1024, 2048))



_NC_CACHE = {}


def _get_nc(name, builder):
    if name not in _NC_CACHE:
        _NC_CACHE[name] = builder()
    return _NC_CACHE[name]


def kernel(**inputs):
    inputs = {k: np.asarray(v) for k, v in inputs.items()}
    cores = list(range(NCORES))

    nc = _get_nc("wprep", build_wprep)
    res = run_bass_kernel_spmd(nc, host_prep_wprep(inputs), core_ids=cores)
    WgT = np.concatenate([np.asarray(r["wg_o"]) for r in res.results], axis=1)
    WuT = np.concatenate([np.asarray(r["wu_o"]) for r in res.results], axis=1)
    WdT = np.concatenate([np.asarray(r["wd_o"]) for r in res.results], axis=1)
    upP = np.concatenate([np.asarray(r["wupP"]) for r in res.results], axis=1)
    gtP = np.concatenate([np.asarray(r["wgtP"]) for r in res.results], axis=1)
    dnP = np.concatenate([np.asarray(r["wdnP"]) for r in res.results], axis=1)

    nc = _get_nc("l0a", build_l0_attn)
    res = run_bass_kernel_spmd(nc, host_prep_l0_attn(inputs), core_ids=cores)
    attnT_full = [np.concatenate([np.asarray(res.results[2 * b]["attnT"]),
                                  np.asarray(res.results[2 * b + 1]["attnT"])], axis=0)
                  for b in range(4)]

    nc = _get_nc("l0f", build_l0_ffn)
    res = run_bass_kernel_spmd(
        nc, host_prep_l0_ffn(inputs, attnT_full, WgT, WuT, WdT), core_ids=cores)
    x2_full = [np.concatenate([np.asarray(res.results[2 * b]["x2T"]),
                               np.asarray(res.results[2 * b + 1]["x2T"])], axis=1)
               for b in range(4)]

    nc = _get_nc("l1a", build_l1_attn)
    res = run_bass_kernel_spmd(nc, host_prep_l1_attn(inputs, x2_full), core_ids=cores)
    ZT_full = [np.concatenate([zraw_to_zt(np.asarray(res.results[2 * b]["Zraw"])),
                               zraw_to_zt(np.asarray(res.results[2 * b + 1]["Zraw"]))],
                              axis=0)
               for b in range(4)]

    nc = _get_nc("l1f", build_l1_ffn)
    maps = host_prep_l1_ffn(inputs, ZT_full, x2_full)
    up, gt, dn = panelize_l1_w(upP, gtP, dnP)
    for m in maps:
        m["wupP"] = up
        m["wgtP"] = gt
        m["wdnP"] = dn
    res = run_bass_kernel_spmd(nc, maps, core_ids=cores)

    out = np.empty((4, S, D), np.float32)
    for b in range(4):
        x4 = np.concatenate([np.asarray(res.results[2 * b]["x4T"]),
                             np.asarray(res.results[2 * b + 1]["x4T"])], axis=1)
        out[b] = x4.T
    return out
